# version 09
# speedup vs baseline: 1.7051x; 1.0581x over previous
import math
import os
import numpy as np
import ml_dtypes
import concourse.bass as bass
import concourse.mybir as mybir
from concourse.bass_utils import run_bass_kernel_spmd
from concourse.bass_core_rust import IndirectOffsetOnAxis
from contextlib import ExitStack

F32 = mybir.dt.float32
BF16 = mybir.dt.bfloat16
AF = mybir.ActivationFunctionType
ALU = mybir.AluOpType
AX = mybir.AxisListType

NCORE = 8
T = 8192
D = 2048
TOWN = 1024
NH = 16
HD = 128
DSTD = 120
RLOW = 8
DFF = 1024
NRT = 16
NSH = 2
CAP = 256
CHUNK_QT = [6, 4, 4, 1, 1]
CHUNK_Q0 = [sum(CHUNK_QT[:i]) for i in range(len(CHUNK_QT))]
NCHUNK = len(CHUNK_QT)
SENT = float(NRT * CAP)
CSC = 1.0 / math.sqrt(HD)
EPS = 1e-5
NEG = -1.0e30

DBG = bool(os.environ.get("KDBG"))

ENGINES = ("pe", "act", "dve", "pool", "sp")


class Prog:

    NLANE = 8

    def __init__(self, nc, sems):
        self.nc = nc
        self.sems = sems
        self.ops = {e: [] for e in ENGINES}
        self.cnt = {e: 0 for e in ENGINES}
        self.dma = {}
        for q in ("sp", "pool"):
            for l in range(self.NLANE):
                self.dma[f"dq_{q}{l}"] = 0
        self.rr = {"sp": 0, "pool": 0}
        self.cc = 0
        self.state = {}
        self.waited = {e: {} for e in ENGINES}

    def _deps(self, eng, reads, writes):
        req = {}

        def add(dep):
            if dep is None:
                return
            s, v = dep
            if req.get(s, 0) < v:
                req[s] = v

        for k in reads:
            st = self.state.get(k)
            if st:
                add(st["w"])
        for k in writes:
            st = self.state.get(k)
            if st:
                add(st["w"])
                for r in st["r"]:
                    add(r)
        out = []
        for s, v in sorted(req.items()):
            if eng == "pe" and s == "pe":
                continue
            if self.waited[eng].get(s, 0) >= v:
                continue
            out.append((s, v))
            self.waited[eng][s] = v
        return out

    def emit(self, eng, fn, reads=(), writes=(), kind="compute"):
        waits = self._deps(eng, reads, writes)
        if kind == "compute":
            self.cnt[eng] += 1
            sem_name, val, inc = eng, self.cnt[eng], 1
        elif kind == "dma":
            q = "sp" if eng == "sp" else "pool"
            lane = self.rr[q]
            self.rr[q] = (lane + 1) % self.NLANE
            ln = f"dq_{q}{lane}"
            prev = self.dma[ln]
            if prev > 0 and self.waited[eng].get(ln, 0) < prev:
                waits.append((ln, prev))
                self.waited[eng][ln] = prev
            self.dma[ln] += 16
            sem_name, val, inc = ln, self.dma[ln], 16
        elif kind == "cc":
            self.cc += 1
            sem_name, val, inc = "cc", self.cc, 1
        else:
            raise ValueError(kind)
        sems = self.sems

        def closure(e, waits=waits, fn=fn, sem_name=sem_name, inc=inc):
            for s, v in waits:
                e.wait_ge(sems[s], v)
            fn(e).then_inc(sems[sem_name], inc)

        self.ops[eng].append(closure)
        dep = (sem_name, val)
        for k in writes:
            self.state[k] = {"w": dep, "r": []}
        for k in reads:
            st = self.state.setdefault(k, {"w": None, "r": []})
            st["r"].append(dep)
        return dep

    def emit_multi(self, eng, fns, reads=(), writes=(), stagger=None):
        waits = self._deps(eng, reads, writes)
        self.cnt[eng] += 1
        sem_name, val = eng, self.cnt[eng]
        sems = self.sems

        def closure(e, waits=waits, fns=fns, sem_name=sem_name):
            for s, v in waits:
                e.wait_ge(sems[s], v)
            for f in fns[:-1]:
                f(e)
            fns[-1](e).then_inc(sems[sem_name], 1)

        self.ops[eng].append(closure)
        dep = (sem_name, val)
        for k in writes:
            self.state[k] = {"w": dep, "r": []}
        for k in reads:
            st = self.state.setdefault(k, {"w": None, "r": []})
            st["r"].append(dep)
        return dep

    def wait_only(self, eng, waits):
        sems = self.sems
        todo = []
        for s, v in waits:
            if self.waited[eng].get(s, 0) >= v:
                continue
            todo.append((s, v))
            self.waited[eng][s] = v
        if not todo:
            return

        def closure(e, todo=todo):
            for s, v in todo:
                e.wait_ge(sems[s], v)

        self.ops[eng].append(closure)

    def all_counts(self):
        snap = [(e, self.cnt[e]) for e in ENGINES if self.cnt[e] > 0]
        snap += [(q, v) for q, v in self.dma.items() if v > 0]
        if self.cc > 0:
            snap.append(("cc", self.cc))
        return snap

    def barrier(self):
        snap = self.all_counts()
        for eng in ENGINES:
            self.wait_only(eng, snap)

    def flush(self, block):
        meth = {"pe": block.tensor, "act": block.scalar, "dve": block.vector,
                "pool": block.gpsimd, "sp": block.sync}
        for eng in ENGINES:
            lst = self.ops[eng]
            if not lst:
                continue

            def runner(e, lst=lst):
                for f in lst:
                    f(e)

            meth[eng](runner)
            self.ops[eng] = []


def build_nc():
    nc = bass.Bass("TRN2", target_bir_lowering=False, debug=False)

    dx = nc.dram_tensor("x", [T, D], F32, kind="ExternalInput")
    dxo = nc.dram_tensor("x_own", [TOWN, D], F32, kind="ExternalInput")
    dwqkv = nc.dram_tensor("wqkv", [D, 768], F32, kind="ExternalInput")
    dwproj = nc.dram_tensor("wproj", [256, D], F32, kind="ExternalInput")
    dwstd = nc.dram_tensor("wstd", [1, 2], F32, kind="ExternalInput")
    dwrec = nc.dram_tensor("wrec", [1, 2], F32, kind="ExternalInput")
    dwshup = nc.dram_tensor("wshup", [NSH, D, DFF], F32, kind="ExternalInput")
    dwshdn = nc.dram_tensor("wshdn", [NSH, DFF, D], F32, kind="ExternalInput")
    dwrtup = nc.dram_tensor("wrtup", [NRT, D, DFF], F32, kind="ExternalInput")
    dwrtdn = nc.dram_tensor("wrtdn", [NRT, DFF, D], F32, kind="ExternalInput")
    dwrout = nc.dram_tensor("wrout", [D, NRT], F32, kind="ExternalInput")
    dident = nc.dram_tensor("ident_bf", [128, 128], BF16, kind="ExternalInput")
    didf = nc.dram_tensor("ident_f", [128, 128], F32, kind="ExternalInput")
    dubar = nc.dram_tensor("ubar", [128, 1], F32, kind="ExternalInput")
    dsel = nc.dram_tensor("sel", [16, D], F32, kind="ExternalInput")
    dmasks = nc.dram_tensor("masks", [4, 128, 512], F32, kind="ExternalInput")
    dy = nc.dram_tensor("y", [TOWN, D], F32, kind="ExternalOutput")
    if DBG:
        ddbg_qf = nc.dram_tensor("dbg_qf", [128, T], BF16, kind="ExternalOutput")
        ddbg_kf = nc.dram_tensor("dbg_kf", [128, T], BF16, kind="ExternalOutput")
        ddbg_vt = nc.dram_tensor("dbg_vt", [128, T], BF16, kind="ExternalOutput")
        ddbg_nb = nc.dram_tensor("dbg_nb", [128, 2], F32, kind="ExternalOutput")
        ddbg_at = nc.dram_tensor("dbg_at", [128, T], BF16, kind="ExternalOutput")
        ddbg_x1 = nc.dram_tensor("dbg_x1", [128, 8 * D], F32, kind="ExternalOutput")
        ddbg_gf = nc.dram_tensor("dbg_gf", [16, 1024], F32, kind="ExternalOutput")
        ddbg_xf2 = nc.dram_tensor("dbg_xf2", [128, 16 * 1024], BF16, kind="ExternalOutput")
        ddbg_mid = nc.dram_tensor("dbg_mid", [128, 8 * 1024], BF16, kind="ExternalOutput")
    dtokio = nc.dram_tensor("tokio", [128, 8], mybir.dt.int32, kind="ExternalInput")
    dlfill = nc.dram_tensor("lfill", [NRT * CAP + 128, 1], mybir.dt.int32, kind="ExternalInput")
    dgzfill = nc.dram_tensor("gzfill", [NRT * CAP + 128, 1], F32, kind="ExternalInput")
    derow = nc.dram_tensor("erow", [16, 1], F32, kind="ExternalInput")
    drs_in = nc.dram_tensor("rs_in", [T, D], BF16)
    drs_out = nc.dram_tensor("rs_out", [TOWN, D], BF16)
    dL = nc.dram_tensor("dL", [NRT * CAP + 128, 1], mybir.dt.int32)
    dG = nc.dram_tensor("dG", [NRT * CAP + 128, 1], F32)
    dxtm = nc.dram_tensor("dxtm", [TOWN + 128, D], BF16)
    doutd = nc.dram_tensor("doutd", [NRT * CAP + 128, D], BF16)

    with ExitStack() as top:
        sems = {}
        sem_names = ["pe", "act", "dve", "pool", "cc"]
        for q in ("sp", "pool"):
            sem_names += [f"dq_{q}{l}" for l in range(Prog.NLANE)]
        for s in sem_names:
            sems[s] = top.enter_context(nc.semaphore(s))
        P = Prog(nc, sems)

        def sb(name, shape, dt=F32, stack=top, side=None):
            kw = {"side": side} if side else {}
            return stack.enter_context(nc.sbuf_tensor(name, shape, dt, **kw))

        pb = [top.enter_context(nc.psum_tensor(f"pb{i}", [128, 512], F32))
              for i in range(8)]

        block = top.enter_context(nc.Block())

        ident = sb("ident", [128, 128], BF16)
        idf = sb("idf", [128, 128], F32)
        ubar = sb("ubar_s", [128, 1], F32)
        onesbf = sb("onesbf", [128, 1], BF16)
        onesf_c = sb("onesf_c", [128, 1], F32)
        onesf_r = sb("onesf_r", [1, 128], F32)
        masks = sb("masks_s", [128, 4 * 512], F32, side="right")
        sqv = sb("sqv", [128, 2], F32)
        tqv = sb("tqv", [128, 2], F32)
        sqq = sb("sqq", [128, 2], F32)
        tqq = sb("tqq", [128, 2], F32)
        nb = sb("nb", [128, 2], F32)
        attn_stack = ExitStack()
        qkv_stack = ExitStack()
        qf = [sb("qf0", [128, T], BF16, qkv_stack), sb("qf1", [128, T], BF16, qkv_stack)]
        kf = [sb("kf0", [128, T], BF16, qkv_stack), sb("kf1", [128, T], BF16, qkv_stack)]
        vt = [sb("vt0", [128, 64 * 129], BF16, qkv_stack),
              sb("vt1", [128, 64 * 129], BF16, qkv_stack)]

        P.emit("sp", lambda e: e.dma_start(out=ident[:], in_=dident.ap()),
               writes=[("ident",)], kind="dma")
        P.emit("sp", lambda e: e.dma_start(out=idf[:], in_=didf.ap()),
               writes=[("idf",)], kind="dma")
        P.emit("sp", lambda e: e.dma_start(out=ubar[:], in_=dubar.ap()),
               writes=[("ubar",)], kind="dma")
        P.emit("sp", lambda e: e.dma_start(
            out=masks[:].rearrange("p (j k) -> p j k", j=4),
            in_=dmasks.ap().rearrange("j p k -> p j k")),
            writes=[("masks",)], kind="dma")
        P.emit("dve", lambda e: e.memset(onesbf[:], 1.0), writes=[("onesbf",)])
        for h in range(2):
            P.emit("dve", lambda e, h=h: e.memset(
                vt[h][:].rearrange("p (s c) -> p s c", c=129)[:, :, 128:129], 1.0),
                writes=[("vtones", h)])
        P.emit("dve", lambda e: e.memset(onesf_c[:], 1.0), writes=[("onesf_c",)])
        P.emit("dve", lambda e: e.memset(onesf_r[:], 1.0), writes=[("onesf_r",)])

        with ExitStack() as ph0:
            wsr = sb("wsr", [1, 2], F32, ph0)
            wrr = sb("wrr", [1, 2], F32, ph0)
            mrow = sb("mrow", [1, 2], F32, ph0)
            cb = sb("cb", [128, 6], F32, ph0)
            tmp1 = sb("cf_t1", [128, 1], F32, ph0)
            tmp2 = sb("cf_t2", [128, 1], F32, ph0)

            P.emit("sp", lambda e: e.dma_start(out=wsr[:], in_=dwstd.ap()),
                   writes=[("wsr",)], kind="dma")
            P.emit("sp", lambda e: e.dma_start(out=wrr[:], in_=dwrec.ap()),
                   writes=[("wrr",)], kind="dma")
            P.emit("dve", lambda e: e.tensor_scalar(mrow[:], wrr[:], 0.1, None, ALU.is_gt),
                   reads=[("wrr",)], writes=[("mrow",)])
            P.emit("dve", lambda e: e.tensor_scalar(wsr[:], wsr[:], 1e-8, None, ALU.max),
                   reads=[("wsr",)], writes=[("wsr",)])
            P.emit("act", lambda e: e.activation(wsr[:], wsr[:], AF.Ln),
                   reads=[("wsr",)], writes=[("wsr",)])
            P.emit("act", lambda e: e.activation(wsr[:], wsr[:], AF.Exp, scale=0.5),
                   reads=[("wsr",)], writes=[("wsr",)])
            P.emit("dve", lambda e: e.tensor_scalar(wrr[:], wrr[:], 1e-8, None, ALU.max),
                   reads=[("wrr",)], writes=[("wrr",)])
            P.emit("act", lambda e: e.activation(wrr[:], wrr[:], AF.Ln),
                   reads=[("wrr",)], writes=[("wrr",)])
            P.emit("act", lambda e: e.activation(wrr[:], wrr[:], AF.Exp, scale=0.5),
                   reads=[("wrr",)], writes=[("wrr",)])
            srcs = [(wsr, 0), (wsr, 1), (wrr, 0), (wrr, 1), (mrow, 0), (mrow, 1)]
            names = [("wsr",), ("wsr",), ("wrr",), ("wrr",), ("mrow",), ("mrow",)]
            for i, ((t, h), nm) in enumerate(zip(srcs, names)):
                P.emit("pe", lambda e, t=t, h=h, i=i: e.matmul(
                    pb[5][:, i:i + 1], lhsT=onesf_r[:], rhs=t[0:1, h:h + 1],
                    start=True, stop=True),
                    reads=[("onesf_r",), nm], writes=[("pb5",)])
            P.emit("dve", lambda e: e.tensor_copy(cb[:], pb[5][:, 0:6]),
                   reads=[("pb5",)], writes=[("cb",)])
            for h in range(2):
                P.emit("dve", lambda e, h=h: e.tensor_tensor(
                    tmp1[:], cb[:, 4 + h:5 + h], ubar[:], ALU.mult),
                    reads=[("cb",), ("ubar",)], writes=[("cft1",)])
                P.emit("dve", lambda e: e.tensor_scalar(
                    tmp2[:], tmp1[:], -1.0, 1.0, ALU.mult, ALU.add),
                    reads=[("cft1",)], writes=[("cft2",)])
                P.emit("dve", lambda e, h=h: e.tensor_tensor(
                    sqv[:, h:h + 1], cb[:, h:h + 1], tmp2[:], ALU.mult),
                    reads=[("cb",), ("cft2",)], writes=[(("sqv", h))])
                P.emit("dve", lambda e, h=h: e.tensor_tensor(
                    tmp1[:], cb[:, 4 + h:5 + h], cb[:, 2 + h:3 + h], ALU.mult),
                    reads=[("cb",)], writes=[("cft1",)])
                P.emit("dve", lambda e, h=h: e.tensor_tensor(
                    tqv[:, h:h + 1], tmp1[:], ubar[:], ALU.mult),
                    reads=[("cft1",), ("ubar",)], writes=[(("tqv", h))])
                P.emit("dve", lambda e, h=h: e.tensor_scalar(
                    sqq[:, h:h + 1], sqv[:, h:h + 1], CSC, None, ALU.mult),
                    reads=[(("sqv", h))], writes=[(("sqq", h))])
                P.emit("dve", lambda e, h=h: e.tensor_scalar(
                    tqq[:, h:h + 1], tqv[:, h:h + 1], CSC, None, ALU.mult),
                    reads=[(("tqv", h))], writes=[(("tqq", h))])

            wqkv_sb = sb("wqkv_sb", [128, 16 * 768], BF16, ph0)
            xf = sb("xf", [128, 16 * 512], BF16, ph0)
            xst = [sb("xst0", [128, D], F32, ph0), sb("xst1", [128, D], F32, ph0),
                   sb("xst2", [128, D], F32, ph0, side="right")]
            xln = [sb("xln0", [128, D], BF16, ph0),
                   sb("xln1", [128, D], BF16, ph0, side="right"),
                   sb("xln2a", [128, D], BF16, ph0, side="right")]
            bnst = [sb(f"bnstA{i}", [128, 24], F32, ph0) for i in range(3)]
            mv = [sb(f"mvA{i}", [128, 2], F32, ph0) for i in range(3)]
            rstd = [sb(f"rstdA{i}", [128, 1], F32, ph0) for i in range(3)]
            bvec = [sb(f"bvecA{i}", [128, 1], F32, ph0) for i in range(3)]
            tmpQ = sb("tmpQ", [128, 512], F32, ph0)
            vtmp = sb("vtmp", [128, 512], BF16, ph0)

            P.emit("pool", lambda e: e.dma_start(
                out=wqkv_sb[:].rearrange("p (kc m) -> p kc m", kc=16),
                in_=dwqkv.ap().rearrange("(kc p) m -> p kc m", p=128)),
                writes=[("wqkv",)], kind="dma")

            NTB = 16
            for tb in range(NTB):
                for tt in range(4):
                    g = 4 * tb + tt
                    sl = g % 3
                    P.emit("sp", lambda e, g=g, sl=sl: e.dma_start(
                        out=xst[sl][:], in_=dx.ap()[g * 128:(g + 1) * 128, :]),
                        writes=[("xst", sl)], kind="dma")
                    for i in range(4):
                        P.emit("dve", lambda e, sl=sl, i=i: e.bn_stats(
                            bnst[sl][:, i * 6:(i + 1) * 6],
                            xst[sl][:, i * 512:(i + 1) * 512]),
                            reads=[("xst", sl)], writes=[("bnst", sl, i)])
                    P.emit("dve", lambda e, sl=sl: e.bn_aggr(mv[sl][:], bnst[sl][:]),
                           reads=[("bnst", sl, 0), ("bnst", sl, 1),
                                  ("bnst", sl, 2), ("bnst", sl, 3)],
                           writes=[("mv", sl)])
                    P.emit("dve", lambda e, sl=sl: e.tensor_scalar(
                        rstd[sl][:], mv[sl][:, 1:2], EPS, None, ALU.add),
                        reads=[("mv", sl)], writes=[("rstd", sl)])
                    P.emit("act", lambda e, sl=sl: e.activation(
                        rstd[sl][:], rstd[sl][:], AF.Ln),
                        reads=[("rstd", sl)], writes=[("rstd", sl)])
                    P.emit("act", lambda e, sl=sl: e.activation(
                        rstd[sl][:], rstd[sl][:], AF.Exp, scale=-0.5),
                        reads=[("rstd", sl)], writes=[("rstd", sl)])
                    P.emit("dve", lambda e, sl=sl: e.tensor_scalar(
                        bvec[sl][:], mv[sl][:, 0:1], -1.0, None, ALU.mult),
                        reads=[("mv", sl)], writes=[("bvec", sl)])
                    P.emit("dve", lambda e, sl=sl: e.tensor_scalar(
                        xln[sl][:], xst[sl][:], bvec[sl][:], rstd[sl][:],
                        ALU.add, ALU.mult),
                        reads=[("xst", sl), ("bvec", sl), ("rstd", sl)],
                        writes=[("xln", sl)])
                    for b4 in range(4):
                        bank = 4 + (b4 % 2)
                        fns = []
                        for ci in range(4):
                            c = b4 * 4 + ci
                            fns.append(lambda e, sl=sl, c=c, ci=ci, bank=bank: e.matmul(
                                pb[bank][:, ci * 128:(ci + 1) * 128],
                                lhsT=xln[sl][:, c * 128:(c + 1) * 128],
                                rhs=ident[:], start=True, stop=True))
                        P.emit_multi("pe", fns, reads=[("xln", sl), ("ident",)],
                                     writes=[(f"pb{bank}",)])
                        dst = (xf[:].rearrange("p (c t) -> p c t", c=16)
                               [:, b4 * 4:(b4 + 1) * 4, tt * 128:(tt + 1) * 128])
                        src = pb[bank][:].rearrange("p (c t) -> p c t", c=4)
                        if b4 % 2 == 0:
                            P.emit("act", lambda e, dst=dst, src=src: e.activation(
                                dst, src, AF.Copy),
                                reads=[(f"pb{bank}",)], writes=[("xf", tb)])
                        else:
                            P.emit("dve", lambda e, dst=dst, src=src: e.tensor_copy(
                                dst, src),
                                reads=[(f"pb{bank}",)], writes=[("xf", tb)])
                morder = [(0, 0), (1, 1), (2, 2), (3, 3), (4, 0), (5, 1)]
                for mi, bank in morder:
                    fns = [lambda e, mi=mi, bank=bank, kc=kc: e.matmul(
                        pb[bank][:],
                        lhsT=wqkv_sb[:, kc * 768 + mi * 128: kc * 768 + (mi + 1) * 128],
                        rhs=xf[:, kc * 512:(kc + 1) * 512],
                        start=(kc == 0), stop=(kc == 15)) for kc in range(16)]
                    P.emit_multi("pe", fns, reads=[("wqkv",), ("xf", tb)],
                                 writes=[(f"pb{bank}",)])
                    if mi in (1, 3):
                        h = 0 if mi == 1 else 1
                        qb, kb = bank - 1, bank
                        ts_ = (tb * 512, (tb + 1) * 512)
                        P.emit("dve", lambda e, qb=qb, h=h: e.tensor_scalar(
                            tmpQ[:], pb[qb][:], sqq[:, h:h + 1], None, ALU.mult),
                            reads=[(f"pb{qb}",), (("sqq", h))], writes=[("tmpQ",)])
                        P.emit("dve", lambda e, kb=kb, h=h, ts_=ts_: e.scalar_tensor_tensor(
                            qf[h][:, ts_[0]:ts_[1]], pb[kb][:], tqq[:, h:h + 1],
                            tmpQ[:], ALU.mult, ALU.add),
                            reads=[(f"pb{kb}",), (("tqq", h)), ("tmpQ",)],
                            writes=[("qf", h, tb)])
                        P.emit("dve", lambda e, kb=kb, h=h: e.tensor_scalar(
                            tmpQ[:], pb[kb][:], sqv[:, h:h + 1], None, ALU.mult),
                            reads=[(f"pb{kb}",), (("sqv", h))], writes=[("tmpQ",)])
                        P.emit("dve", lambda e, qb=qb, h=h, ts_=ts_: e.scalar_tensor_tensor(
                            kf[h][:, ts_[0]:ts_[1]], pb[qb][:], tqv[:, h:h + 1],
                            tmpQ[:], ALU.mult, ALU.add),
                            reads=[(f"pb{qb}",), (("tqv", h)), ("tmpQ",)],
                            writes=[("kf", h, tb)])
                    if mi in (4, 5):
                        h = mi - 4
                        P.emit("act", lambda e, bank=bank: e.activation(
                            vtmp[:], pb[bank][:], AF.Copy),
                            reads=[(f"pb{bank}",)], writes=[("vtmp",)])
                        tbank = 6
                        fns = [lambda e, tt=tt, tbank=tbank: e.matmul(
                            pb[tbank][:, tt * 128:(tt + 1) * 128],
                            lhsT=vtmp[:, tt * 128:(tt + 1) * 128],
                            rhs=ident[:], start=True, stop=True) for tt in range(4)]
                        P.emit_multi("pe", fns, reads=[("vtmp",), ("ident",)],
                                     writes=[(f"pb{tbank}",)])
                        P.emit("dve", lambda e, h=h, tb=tb, tbank=tbank: e.tensor_copy(
                            vt[h][:].rearrange("p (s c) -> p s c", c=129)
                            [:, 4 * tb:4 * tb + 4, 0:128],
                            pb[tbank][:].rearrange("p (s c) -> p s c", c=128)),
                            reads=[(f"pb{tbank}",)], writes=[("vt", h, tb)])

            P.barrier()
            P.flush(block)

        with ExitStack() as phnb:
            sqt = sb("sqt", [128, 512], F32, phnb)
            nrow = sb("nrow", [1, 512], F32, phnb)
            nmax = sb("nmax", [1, 4], F32, phnb)
            nsc = sb("nsc", [1, 1], F32, phnb)
            P.emit("dve", lambda e: e.memset(nmax[:], 0.0), writes=[("nmax",)])
            for h in range(2):
                for which, src in ((0, qf[h]), (1, kf[h])):
                    col = 2 * h + which
                    for i in range(16):
                        P.emit("act", lambda e, src=src, i=i: e.activation(
                            sqt[:], src[:, i * 512:(i + 1) * 512], AF.Square),
                            reads=[("qf", h, i) if src is qf[h] else ("kf", h, i)],
                            writes=[("sqt",)])
                        P.emit("pe", lambda e: e.matmul(
                            pb[7][0:1, :], lhsT=onesf_c[:], rhs=sqt[:],
                            start=True, stop=True),
                            reads=[("onesf_c",), ("sqt",)], writes=[("pb7",)])
                        P.emit("dve", lambda e: e.tensor_copy(nrow[:], pb[7][0:1, :]),
                               reads=[("pb7",)], writes=[("nrow",)])
                        P.emit("dve", lambda e: e.tensor_reduce(
                            nsc[:], nrow[:], axis=AX.X, op=ALU.max),
                            reads=[("nrow",)], writes=[("nsc",)])
                        P.emit("dve", lambda e, col=col: e.tensor_tensor(
                            nmax[:, col:col + 1], nmax[:, col:col + 1], nsc[:], ALU.max),
                            reads=[("nmax",), ("nsc",)], writes=[("nmax",)])
            for h in range(2):
                P.emit("dve", lambda e, h=h: e.tensor_tensor(
                    nsc[:], nmax[:, 2 * h:2 * h + 1], nmax[:, 2 * h + 1:2 * h + 2],
                    ALU.mult), reads=[("nmax",)], writes=[("nsc",)])
                P.emit("dve", lambda e: e.tensor_scalar(
                    nsc[:], nsc[:], 1e-30, None, ALU.max),
                    reads=[("nsc",)], writes=[("nsc",)])
                P.emit("act", lambda e: e.activation(nsc[:], nsc[:], AF.Ln),
                       reads=[("nsc",)], writes=[("nsc",)])
                P.emit("act", lambda e: e.activation(nsc[:], nsc[:], AF.Exp, scale=0.5),
                       reads=[("nsc",)], writes=[("nsc",)])
                P.emit("dve", lambda e: e.tensor_scalar(
                    nsc[:], nsc[:], -1.0, None, ALU.mult),
                    reads=[("nsc",)], writes=[("nsc",)])
                P.emit("pe", lambda e, h=h: e.matmul(
                    pb[7][:, h:h + 1], lhsT=onesf_r[:], rhs=nsc[:],
                    start=True, stop=True),
                    reads=[("onesf_r",), ("nsc",)], writes=[("pb7",)])
            P.emit("dve", lambda e: e.tensor_copy(nb[:], pb[7][:, 0:2]),
                   reads=[("pb7",)], writes=[("nb",)])

            P.barrier()
            P.flush(block)

        attn = [sb("attn0", [128, T], BF16, attn_stack, side="right"),
                sb("attn1", [128, T], BF16, attn_stack, side="right")]

        with ExitStack() as ph2:
            if DBG:
                P.emit("sp", lambda e: e.dma_start(out=ddbg_qf.ap(), in_=qf[0][:]),
                       reads=[("qf", 0, i) for i in range(16)], kind="dma")
                P.emit("sp", lambda e: e.dma_start(out=ddbg_kf.ap(), in_=kf[0][:]),
                       reads=[("kf", 0, i) for i in range(16)], kind="dma")
                P.emit("sp", lambda e: e.dma_start(out=ddbg_nb.ap(), in_=nb[:]),
                       reads=[("nb",)], kind="dma")
            pt = [sb(f"pt{i}", [128, 512], BF16, ph2) for i in range(3)]
            atm = sb("atm", [128, 512], BF16, ph2)
            lnl = sb("lnl", [128, 1], F32, ph2)
            rlq = sb("rlq", [128, 1], F32, ph2)
            wp_sb = sb("wp_sb", [128, 2 * D], BF16, ph2)
            rsst = [sb(f"rsst{i}", [128, D], BF16, ph2) for i in range(4)]
            P.emit("pool", lambda e: e.dma_start(
                out=wp_sb[:].rearrange("p (hc o) -> p hc o", hc=2),
                in_=dwproj.ap().rearrange("(hc p) o -> p hc o", p=128)),
                writes=[("wp",)], kind="dma")
            for qt in range(16):
                nkt = 4 * (qt + 1)
                for h in range(2):
                    qkeys = [("qf", h, i) for i in range(16)]
                    kkeys = [("kf", h, i) for i in range(16)]
                    vkeys = [("vt", h, i) for i in range(16)]

                    def emit_S(kt, h=h, qt=qt, nkt=nkt, qkeys=qkeys, kkeys=kkeys):
                        sl = kt % 3
                        P.emit("pe", lambda e, h=h, qt=qt, kt=kt, sl=sl: e.matmul(
                            pb[sl][:],
                            lhsT=kf[h][:, kt * 128:(kt + 1) * 128],
                            rhs=qf[h][:, qt * 512:(qt + 1) * 512],
                            start=True, stop=True),
                            reads=[kkeys[kt // 4], qkeys[qt]],
                            writes=[(f"pb{sl}",)])
                        j = kt - (nkt - 4)
                        if j >= 0:
                            P.emit("dve", lambda e, sl=sl, j=j: e.tensor_tensor(
                                pb[sl][:], pb[sl][:],
                                masks[:, j * 512:(j + 1) * 512], ALU.add),
                                reads=[(f"pb{sl}",), ("masks",)],
                                writes=[(f"pb{sl}",)])

                    emit_S(0)
                    if nkt > 1:
                        emit_S(1)
                    for kt in range(nkt):
                        sl = kt % 3
                        if kt + 2 < nkt:
                            emit_S(kt + 2)
                        P.emit("act", lambda e, sl=sl, h=h: e.activation(
                            pt[sl][:], pb[sl][:], AF.Exp,
                            bias=nb[:, h:h + 1]),
                            reads=[(f"pb{sl}",), ("nb",)], writes=[("pt", sl)])
                        fns = [lambda e, h=h, kt=kt, sl=sl, qs=qs,
                               first=(kt == 0), last=(kt == nkt - 1): e.matmul(
                            pb[3 + qs][:, 0:129],
                            lhsT=pt[sl][:, qs * 128:(qs + 1) * 128],
                            rhs=vt[h][:, kt * 129:(kt + 1) * 129],
                            start=first, stop=last) for qs in range(4)]
                        P.emit_multi("pe", fns,
                                     reads=[vkeys[kt // 4], ("vtones", h), ("pt", sl)],
                                     writes=[(f"pb{3 + qs}",) for qs in range(4)])
                    for qs in range(4):
                        P.emit("act", lambda e, qs=qs: e.activation(
                            lnl[:], pb[3 + qs][:, 128:129], AF.Ln),
                            reads=[(f"pb{3 + qs}",)], writes=[("lnl",)])
                        P.emit("act", lambda e: e.activation(
                            rlq[:], lnl[:], AF.Exp, scale=-1.0),
                            reads=[("lnl",)], writes=[("rlq",)])
                        P.emit("act", lambda e, qs=qs: e.activation(
                            atm[:, qs * 128:(qs + 1) * 128], pb[3 + qs][:, 0:128],
                            AF.Identity, scale=rlq[:]),
                            reads=[(f"pb{3 + qs}",), ("rlq",)],
                            writes=[("atm", qs)])
                    fns = [lambda e, ci=ci: e.matmul(
                        pb[7][:, ci * 128:(ci + 1) * 128],
                        lhsT=atm[:, ci * 128:(ci + 1) * 128],
                        rhs=ident[:], start=True, stop=True) for ci in range(4)]
                    P.emit_multi("pe", fns,
                                 reads=[("atm", 0), ("atm", 1), ("atm", 2),
                                        ("atm", 3), ("ident",)],
                                 writes=[("pb7",)])
                    P.emit("act", lambda e, h=h, qt=qt: e.activation(
                        attn[h][:, qt * 512:(qt + 1) * 512], pb[7][:], AF.Copy),
                        reads=[("pb7",)], writes=[("attn", h, qt)])
                for g in range(4 * qt, 4 * qt + 4):
                    bks = (3, 4, 5, 6) if g % 2 == 0 else (0, 1, 2, 7)
                    sl = g % 4
                    fns = []
                    for h in range(2):
                        for og in range(4):
                            fns.append(lambda e, g=g, h=h, og=og, b=bks[og]: e.matmul(
                                pb[b][:],
                                lhsT=attn[h][:, g * 128:(g + 1) * 128],
                                rhs=wp_sb[:, h * D + og * 512: h * D + (og + 1) * 512],
                                start=(h == 0), stop=(h == 1)))
                    P.emit_multi("pe", fns,
                                 reads=[("attn", 0, g // 4), ("attn", 1, g // 4),
                                        ("wp",)],
                                 writes=[(f"pb{b}",) for b in bks])
                    for og in range(4):
                        P.emit("dve", lambda e, og=og, sl=sl, b=bks[og]: e.tensor_copy(
                            rsst[sl][:, og * 512:(og + 1) * 512], pb[b][:]),
                            reads=[(f"pb{bks[og]}",)], writes=[("rsst", sl, og)])
                    kch = next(i for i in range(NCHUNK)
                               if CHUNK_Q0[i] <= g // 4 < CHUNK_Q0[i] + CHUNK_QT[i])
                    P.emit("pool", lambda e, g=g, sl=sl: e.dma_start(
                        out=drs_in.ap()[g * 128:(g + 1) * 128, :], in_=rsst[sl][:]),
                        reads=[("rsst", sl, 0), ("rsst", sl, 1), ("rsst", sl, 2),
                               ("rsst", sl, 3)],
                        writes=[("rsch", kch)], kind="dma")
                for k in range(NCHUNK):
                    if qt == CHUNK_Q0[k] + CHUNK_QT[k] - 1:
                        r0 = 512 * CHUNK_Q0[k]
                        rn = 512 * CHUNK_QT[k]
                        o0 = 64 * CHUNK_Q0[k]
                        on = 64 * CHUNK_QT[k]
                        P.emit("pool", lambda e, r0=r0, rn=rn, o0=o0, on=on: e.collective_compute(
                            "ReduceScatter", ALU.add,
                            replica_groups=[list(range(NCORE))],
                            ins=[drs_in.ap()[r0:r0 + rn, :].opt()],
                            outs=[drs_out.ap()[o0:o0 + on, :].opt()]),
                            reads=[("rsch", k)], writes=[("rs_out", k)], kind="cc")
            if DBG:
                P.emit("sp", lambda e: e.dma_start(out=ddbg_at.ap(), in_=attn[0][:]),
                       reads=[("attn", 0, i) for i in range(16)], kind="dma")
            P.barrier()
            P.flush(block)
        qkv_stack.close()
        attn_stack.close()

        with ExitStack() as ph5:
            x1 = sb("x1", [128, 8 * D], F32, ph5)
            phres = ExitStack()
            rtmp0 = sb("rtmp0", [128, D], BF16, phres)
            rtmp = [rtmp0, rtmp0]
            for i in range(8):
                sl = 0
                P.emit("sp", lambda e, i=i: e.dma_start(
                    out=x1[:, i * D:(i + 1) * D],
                    in_=dxo.ap()[i * 128:(i + 1) * 128, :]),
                    writes=[("x1", i)], kind="dma")
                rs_keys = [("rs_out", k) for k in range(NCHUNK)
                           if 64 * CHUNK_Q0[k] < (i + 1) * 128
                           and 64 * (CHUNK_Q0[k] + CHUNK_QT[k]) > i * 128]
                P.emit("sp", lambda e, i=i, sl=sl: e.dma_start(
                    out=rtmp[sl][:], in_=drs_out.ap()[i * 128:(i + 1) * 128, :]),
                    reads=rs_keys, writes=[("rtmp", sl)], kind="dma")
                P.emit("dve", lambda e, i=i, sl=sl: e.tensor_tensor(
                    x1[:, i * D:(i + 1) * D], x1[:, i * D:(i + 1) * D],
                    rtmp[sl][:], ALU.add),
                    reads=[("x1", i), ("rtmp", sl)], writes=[("x1", i)])

            phres.close()

            xf2 = sb("xf2", [128, 16 * 1024], BF16, ph5)
            xln2 = [sb("xln20", [128, D], BF16, ph5),
                    sb("xln21", [128, D], BF16, ph5)]
            bnst2 = sb("bnst2", [128, 24], F32, ph5)
            mv2 = sb("mv2", [128, 2], F32, ph5)
            nmean2 = sb("nmean2", [128, 1], F32, ph5)
            rstd2 = sb("rstd2", [128, 1], F32, ph5)
            for i in range(8):
                sl = i % 2
                for j in range(4):
                    P.emit("dve", lambda e, i=i, j=j: e.bn_stats(
                        bnst2[:, j * 6:(j + 1) * 6],
                        x1[:, i * D + j * 512: i * D + (j + 1) * 512]),
                        reads=[("x1", i)], writes=[("bnst2", j)])
                P.emit("dve", lambda e: e.bn_aggr(mv2[:], bnst2[:]),
                       reads=[("bnst2", 0), ("bnst2", 1), ("bnst2", 2), ("bnst2", 3)],
                       writes=[("mv2",)])
                P.emit("dve", lambda e: e.tensor_scalar(
                    nmean2[:], mv2[:, 0:1], -1.0, None, ALU.mult),
                    reads=[("mv2",)], writes=[("nmean2",)])
                P.emit("dve", lambda e: e.tensor_scalar(
                    rstd2[:], mv2[:, 1:2], EPS, None, ALU.add),
                    reads=[("mv2",)], writes=[("rstd2",)])
                P.emit("act", lambda e: e.activation(
                    rstd2[:], rstd2[:], AF.Ln),
                    reads=[("rstd2",)], writes=[("rstd2",)])
                P.emit("act", lambda e: e.activation(
                    rstd2[:], rstd2[:], AF.Exp, scale=-0.5),
                    reads=[("rstd2",)], writes=[("rstd2",)])
                P.emit("dve", lambda e, i=i, sl=sl: e.tensor_scalar(
                    xln2[sl][:], x1[:, i * D:(i + 1) * D], nmean2[:], rstd2[:],
                    ALU.add, ALU.mult),
                    reads=[("x1", i), ("nmean2",), ("rstd2",)],
                    writes=[("xln2", sl)])
                P.emit("sp", lambda e, i=i, sl=sl: e.dma_start(
                    out=dxtm.ap()[i * 128:(i + 1) * 128, :], in_=xln2[sl][:]),
                    reads=[("xln2", sl)], writes=[("dxtm", i)], kind="dma")
                for b4 in range(4):
                    bank = 0 + (b4 % 2)
                    fns = [lambda e, sl=sl, c=b4 * 4 + ci, ci=ci, bank=bank: e.matmul(
                        pb[bank][:, ci * 128:(ci + 1) * 128],
                        lhsT=xln2[sl][:, c * 128:(c + 1) * 128],
                        rhs=ident[:], start=True, stop=True) for ci in range(4)]
                    P.emit_multi("pe", fns, reads=[("xln2", sl), ("ident",)],
                                 writes=[(f"pb{bank}",)])
                    P.emit("dve", lambda e, b4=b4, i=i, bank=bank: e.tensor_copy(
                        xf2[:].rearrange("p (c t) -> p c t", c=16)
                        [:, b4 * 4:(b4 + 1) * 4, i * 128:(i + 1) * 128],
                        pb[bank][:].rearrange("p (c t) -> p c t", c=4)),
                        reads=[(f"pb{bank}",)], writes=[("xf2", i)])
            xf2keys = [("xf2", i) for i in range(8)]

            outg = sb("outg", [128, D], BF16, ph5)
            oh1 = sb("oh1", [128, 8 * 16], F32, ph5)
            oh2 = sb("oh2", [128, 8 * 16], F32, ph5)
            gv = sb("gv", [128, 16], F32, ph5)
            g1i = sb("g1i", [128, 8], mybir.dt.int32, ph5)
            g2i = sb("g2i", [128, 8], mybir.dt.int32, ph5)
            tokio_s = sb("tokio_s", [128, 8], mybir.dt.int32, ph5)
            P.emit("dve", lambda e: e.memset(outg[:], 0.0), writes=[("outg", og) for og in range(4)])
            P.emit("sp", lambda e: e.dma_start(
                out=dxtm.ap()[TOWN:TOWN + 128, :], in_=outg[:]),
                reads=[("outg", og) for og in range(4)],
                writes=[("dxtm_trash",)], kind="dma")
            P.emit("sp", lambda e: e.dma_start(
                out=doutd.ap()[NRT * CAP:NRT * CAP + 128, :], in_=outg[:]),
                reads=[("outg", og) for og in range(4)],
                writes=[("outd_trash",)], kind="dma")
            phr = ExitStack()
            wrr_sb = sb("wrr_sb", [128, 16 * 16], BF16, phr)
            lgf = sb("lgf", [16, 1024], F32, phr)
            lgt = sb("lgt", [128, 16], F32, phr)
            prb = sb("prb", [128, 16], F32, phr)
            ge = sb("ge", [128, 16], F32, phr)
            rmax = sb("rmax", [128, 1], F32, phr)
            rsum = sb("rsum", [128, 1], F32, phr)
            mkfm = sb("mkfm", [16, 1024], F32, phr)
            posfm = sb("posfm", [16, 1024], F32, phr)
            slotfm = sb("slotfm", [16, 1024], F32, phr)
            mkok = sb("mkok", [16, 1024], F32, phr)
            erow_s = sb("erow_s", [16, 1], F32, phr)
            slottm = sb("slottm", [128, 16], F32, phr)
            g1f = sb("g1f", [128, 1], F32, phr)
            P.emit("sp", lambda e: e.dma_start(out=erow_s[:], in_=derow.ap()),
                   writes=[("erow",)], kind="dma")
            P.emit("sp", lambda e: e.dma_start(out=tokio_s[:], in_=dtokio.ap()),
                   writes=[("tokio",)], kind="dma")
            P.emit("pool", lambda e: e.dma_start(out=dL.ap(), in_=dlfill.ap()),
                   writes=[("dL",)], kind="dma")
            P.emit("pool", lambda e: e.dma_start(out=dG.ap(), in_=dgzfill.ap()),
                   writes=[("dG",)], kind="dma")
            P.emit("pool", lambda e: e.dma_start(
                out=wrr_sb[:].rearrange("p (kc m) -> p kc m", kc=16),
                in_=dwrout.ap().rearrange("(kc p) m -> p kc m", p=128)),
                writes=[("wrr_sb",)], kind="dma")

            for tq in range(2):
                bank = 2 + tq
                for kc in range(16):
                    P.emit("pe", lambda e, tq=tq, kc=kc, bank=bank: e.matmul(
                        pb[bank][0:16, :],
                        lhsT=wrr_sb[:, kc * 16:(kc + 1) * 16],
                        rhs=xf2[:, kc * 1024 + tq * 512: kc * 1024 + (tq + 1) * 512],
                        start=(kc == 0), stop=(kc == 15)),
                        reads=[("wrr_sb",)] + xf2keys, writes=[(f"pb{bank}",)])
                P.emit("dve", lambda e, tq=tq, bank=bank: e.tensor_copy(
                    lgf[:, tq * 512:(tq + 1) * 512], pb[bank][0:16, :]),
                    reads=[(f"pb{bank}",)], writes=[("lgf", tq)])
            for tb8 in range(8):
                P.emit("pe", lambda e, tb8=tb8: e.transpose(
                    pb[4][:, 0:16], in_=lgf[:, tb8 * 128:(tb8 + 1) * 128],
                    identity=idf[0:16, 0:16]),
                    reads=[("lgf", tb8 // 4), ("idf",)], writes=[("pb4",)])
                P.emit("dve", lambda e: e.tensor_copy(lgt[:], pb[4][:, 0:16]),
                       reads=[("pb4",)], writes=[("lgt",)])
                P.emit("dve", lambda e: e.tensor_reduce(
                    rmax[:], lgt[:], axis=AX.X, op=ALU.max),
                    reads=[("lgt",)], writes=[("rmax",)])
                P.emit("dve", lambda e: e.tensor_scalar(
                    rmax[:], rmax[:], -1.0, None, ALU.mult),
                    reads=[("rmax",)], writes=[("rmax",)])
                P.emit("act", lambda e: e.activation(
                    prb[:], lgt[:], AF.Exp, bias=rmax[:]),
                    reads=[("lgt",), ("rmax",)], writes=[("prb",)])
                P.emit("dve", lambda e: e.tensor_reduce(
                    rsum[:], prb[:], axis=AX.X, op=ALU.add),
                    reads=[("prb",)], writes=[("rsum",)])
                P.emit("dve", lambda e: e.reciprocal(rsum[:], rsum[:]),
                       reads=[("rsum",)], writes=[("rsum",)])
                P.emit("dve", lambda e: e.tensor_scalar(
                    prb[:], prb[:], rsum[:], None, ALU.mult),
                    reads=[("prb",), ("rsum",)], writes=[("prb",)])
                o1 = (slice(None), slice(tb8 * 16, (tb8 + 1) * 16))
                P.emit("dve", lambda e: e.tensor_reduce(
                    rmax[:], prb[:], axis=AX.X, op=ALU.max),
                    reads=[("prb",)], writes=[("rmax",)])
                P.emit("dve", lambda e, o1=o1: e.tensor_scalar(
                    oh1[o1], prb[:], rmax[:], None, ALU.is_ge),
                    reads=[("prb",), ("rmax",)], writes=[("oh1", tb8)])
                P.emit("dve", lambda e, o1=o1: e.scalar_tensor_tensor(
                    ge[:], oh1[o1], NEG, prb[:], ALU.mult, ALU.add),
                    reads=[("oh1", tb8), ("prb",)], writes=[("ge",)])
                P.emit("dve", lambda e: e.tensor_reduce(
                    rmax[:], ge[:], axis=AX.X, op=ALU.max),
                    reads=[("ge",)], writes=[("rmax",)])
                P.emit("dve", lambda e: e.tensor_scalar(
                    ge[:], prb[:], rmax[:], None, ALU.is_ge),
                    reads=[("prb",), ("rmax",)], writes=[("ge",)])
                P.emit("dve", lambda e, o1=o1: e.tensor_tensor(
                    oh2[o1], ge[:], oh1[o1], ALU.subtract),
                    reads=[("ge",), ("oh1", tb8)], writes=[("oh2", tb8)])
                P.emit("dve", lambda e, o1=o1: e.tensor_tensor(
                    lgt[:], prb[:], oh1[o1], ALU.mult),
                    reads=[("prb",), ("oh1", tb8)], writes=[("lgt",)])
                P.emit("dve", lambda e, tb8=tb8: e.tensor_reduce(
                    gv[:, tb8:tb8 + 1], lgt[:], axis=AX.X, op=ALU.add),
                    reads=[("lgt",)], writes=[("gv", tb8, 0)])
                P.emit("dve", lambda e, o1=o1: e.tensor_tensor(
                    lgt[:], prb[:], oh2[o1], ALU.mult),
                    reads=[("prb",), ("oh2", tb8)], writes=[("lgt",)])
                P.emit("dve", lambda e, tb8=tb8: e.tensor_reduce(
                    gv[:, 8 + tb8:9 + tb8], lgt[:], axis=AX.X, op=ALU.add),
                    reads=[("lgt",)], writes=[("gv", tb8, 1)])
                P.emit("pe", lambda e: e.transpose(
                    pb[5][0:16, 0:128], in_=ge[:], identity=idf[:]),
                    reads=[("ge",), ("idf",)], writes=[("pb5",)])
                P.emit("dve", lambda e, tb8=tb8: e.tensor_copy(
                    mkfm[:, tb8 * 128:(tb8 + 1) * 128], pb[5][0:16, 0:128]),
                    reads=[("pb5",)], writes=[("mkfm",)])

            P.emit("dve", lambda e: e.tensor_tensor_scan(
                posfm[:], mkfm[:], mkfm[:], 0.0, ALU.add, ALU.bypass),
                reads=[("mkfm",)], writes=[("posfm",)])
            P.emit("dve", lambda e: e.tensor_scalar(
                slotfm[:], posfm[:], erow_s[:], None, ALU.add),
                reads=[("posfm",), ("erow",)], writes=[("slotfm",)])
            P.emit("dve", lambda e: e.tensor_scalar(
                mkok[:], posfm[:], float(CAP), None, ALU.is_le),
                reads=[("posfm",)], writes=[("mkok",)])
            P.emit("dve", lambda e: e.tensor_scalar(
                slotfm[:], slotfm[:], -SENT, None, ALU.add),
                reads=[("slotfm",)], writes=[("slotfm",)])
            P.emit("dve", lambda e: e.tensor_tensor(
                slotfm[:], slotfm[:], mkok[:], ALU.mult),
                reads=[("slotfm",), ("mkok",)], writes=[("slotfm",)])
            P.emit("dve", lambda e: e.tensor_scalar(
                slotfm[:], slotfm[:], SENT, None, ALU.add),
                reads=[("slotfm",)], writes=[("slotfm",)])
            for tb8 in range(8):
                P.emit("pe", lambda e, tb8=tb8: e.transpose(
                    pb[4][:, 0:16], in_=slotfm[:, tb8 * 128:(tb8 + 1) * 128],
                    identity=idf[0:16, 0:16]),
                    reads=[("slotfm",), ("idf",)], writes=[("pb4",)])
                P.emit("dve", lambda e: e.tensor_copy(slottm[:], pb[4][:, 0:16]),
                       reads=[("pb4",)], writes=[("slottm",)])
                for ki, ohb in ((0, oh1), (1, oh2)):
                    gi = g1i if ki == 0 else g2i
                    P.emit("dve", lambda e, tb8=tb8, ohb=ohb: e.tensor_tensor(
                        lgt[:], slottm[:], ohb[:, tb8 * 16:(tb8 + 1) * 16],
                        ALU.mult),
                        reads=[("slottm",), (("oh1" if ohb is oh1 else "oh2"), tb8)],
                        writes=[("lgt",)])
                    P.emit("dve", lambda e: e.tensor_reduce(
                        g1f[:], lgt[:], axis=AX.X, op=ALU.add),
                        reads=[("lgt",)], writes=[("g1f",)])
                    P.emit("dve", lambda e, tb8=tb8, gi=gi: e.tensor_copy(
                        gi[:, tb8:tb8 + 1], g1f[:]),
                        reads=[("g1f",)], writes=[(("g1i" if gi is g1i else "g2i"), tb8)])
            for tb8 in range(8):
                for ki in range(2):
                    gi = g1i if ki == 0 else g2i
                    gkey = ("g1i" if ki == 0 else "g2i", tb8)
                    P.emit("pool", lambda e, tb8=tb8, gi=gi: e.indirect_dma_start(
                        out=dL.ap(), out_offset=IndirectOffsetOnAxis(gi[:, tb8:tb8 + 1], 0),
                        in_=tokio_s[:, tb8:tb8 + 1], in_offset=None),
                        reads=[gkey, ("tokio",)], writes=[("dL",)], kind="dma")
                    P.emit("pool", lambda e, tb8=tb8, gi=gi, ki=ki: e.indirect_dma_start(
                        out=dG.ap(), out_offset=IndirectOffsetOnAxis(gi[:, tb8:tb8 + 1], 0),
                        in_=gv[:, 8 * ki + tb8: 8 * ki + tb8 + 1], in_offset=None),
                        reads=[gkey, ("gv", tb8, ki)], writes=[("dG",)], kind="dma")

            if DBG:
                P.emit("sp", lambda e: e.dma_start(out=ddbg_x1.ap(), in_=x1[:]),
                       reads=[("x1", i) for i in range(8)], kind="dma")
                P.emit("sp", lambda e: e.dma_start(out=ddbg_xf2.ap(), in_=xf2[:]),
                       reads=xf2keys, kind="dma")


            phr.close()

            mid = sb("mid", [128, 8 * 1024], BF16, ph5)
            rmid = sb("rmid", [128, 8 * CAP], BF16, ph5)
            ups = [sb("ups0", [128, 16 * 128], BF16, ph5),
                   sb("ups1", [128, 16 * 128], BF16, ph5),
                   sb("ups2", [128, 16 * 128], BF16, ph5)]
            dns = sb("dns", [128, 8 * D], BF16, ph5)
            gsc0 = sb("gsc0", [128, 512], F32, ph5)
            gsc = [gsc0, gsc0]
            grow = [sb("grow0", [1, CAP], F32, ph5), sb("grow1", [1, CAP], F32, ph5)]
            gb = [sb("gb0", [128, CAP], F32, ph5), sb("gb1", [128, CAP], F32, ph5)]
            gxt = [sb("gxt0", [128, D], BF16, ph5), sb("gxt1", [128, D], BF16, ph5)]
            gxf = sb("gxf", [128, 16 * CAP], BF16, ph5)
            li = [sb("li0", [128, 1], mybir.dt.int32, ph5),
                  sb("li1", [128, 1], mybir.dt.int32, ph5)]

            def gather_fetch(rr):
                for ct in range(CAP // 128):
                    csl = (rr * (CAP // 128) + ct) % 2
                    P.emit("sp", lambda e, rr=rr, ct=ct, csl=csl: e.dma_start(
                        out=li[csl][:],
                        in_=dL.ap()[rr * CAP + ct * 128: rr * CAP + (ct + 1) * 128, :]),
                        reads=[("dL",)], writes=[("li", csl)], kind="dma")
                    P.emit("pool", lambda e, csl=csl: e.indirect_dma_start(
                        out=gxt[csl][:], out_offset=None,
                        in_=dxtm.ap(),
                        in_offset=IndirectOffsetOnAxis(li[csl][:], 0)),
                        reads=[("li", csl), ("dxtm_trash",)]
                        + [("dxtm", i) for i in range(8)],
                        writes=[("gxt", csl)], kind="dma")

            def gate_fetch(r):
                gsl = r % 2
                P.emit("sp", lambda e, r=r, gsl=gsl: e.dma_start(
                    out=grow[gsl][:],
                    in_=dG.ap()[r * CAP:(r + 1) * CAP, :].rearrange("c x -> x c")),
                    reads=[("dG",)], writes=[("grow", gsl)], kind="dma")
                P.emit("pe", lambda e, gsl=gsl: e.matmul(
                    pb[7][:, 0:CAP], lhsT=onesf_r[:], rhs=grow[gsl][:],
                    start=True, stop=True),
                    reads=[("onesf_r",), ("grow", gsl)], writes=[("pb7",)])
                P.emit("dve", lambda e, gsl=gsl: e.tensor_copy(
                    gb[gsl][:], pb[7][:, 0:CAP]),
                    reads=[("pb7",)], writes=[("gb", gsl)])

            def up_dma(e_, mc, sl):
                if e_ < NSH:
                    src = dwshup.ap()[e_]
                else:
                    src = dwrtup.ap()[e_ - NSH]
                src = src.rearrange("(hc p) m -> p hc m", p=128)[:, :, mc * 128:(mc + 1) * 128]
                P.emit("pool", lambda e, src=src, sl=sl: e.dma_start(
                    out=ups[sl][:].rearrange("p (hc m) -> p hc m", hc=16), in_=src),
                    writes=[("ups", sl)], kind="dma")

            midkeys_all = [("mid", mc, hf) for mc in range(8) for hf in range(2)]

            def dn_dma(e_, mc):
                if e_ < NSH:
                    src = dwshdn.ap()[e_]
                    P.emit("pool", lambda e, src=src, mc=mc: e.dma_start(
                        out=dns[:, mc * D:(mc + 1) * D],
                        in_=src[mc * 128:(mc + 1) * 128, :]),
                        writes=[("dns", mc)], kind="dma")
                else:
                    src = dwrtdn.ap()[e_ - NSH]
                    sl = mc % 2
                    stg = mid[:].bitcast(F32)[:, sl * D:(sl + 1) * D]
                    P.emit("sp", lambda e, src=src, mc=mc, stg=stg: e.dma_start(
                        out=stg, in_=src[mc * 128:(mc + 1) * 128, :]),
                        writes=[("dnstg", sl)] + midkeys_all, kind="dma")
                    P.emit("act", lambda e, mc=mc, stg=stg: e.activation(
                        dns[:, mc * D:(mc + 1) * D], stg, AF.Copy),
                        reads=[("dnstg", sl)], writes=[("dns", mc)])

            NEXP = NSH + NRT
            upsched = [(e_, mc) for e_ in range(NEXP) for mc in range(8)]
            up_dma(0, 0, 0)
            up_dma(0, 1, 1)
            up_dma(0, 2, 2)
            gate_fetch(0)
            gather_fetch(0)
            for ex in range(NEXP):
                routed = ex >= NSH
                r = ex - NSH
                gsl = r % 2 if routed else 0
                if routed:
                    for ct in range(CAP // 128):
                        csl = (r * (CAP // 128) + ct) % 2
                        for b4 in range(4):
                            bank = 2 + (b4 % 2)
                            fns = [lambda e, csl=csl, c=b4 * 4 + ci, ci=ci, bank=bank: e.matmul(
                                pb[bank][:, ci * 128:(ci + 1) * 128],
                                lhsT=gxt[csl][:, c * 128:(c + 1) * 128],
                                rhs=ident[:], start=True, stop=True)
                                for ci in range(4)]
                            P.emit_multi("pe", fns,
                                         reads=[("gxt", csl), ("ident",)],
                                         writes=[(f"pb{bank}",)])
                            dst = (gxf[:].rearrange("p (hc c) -> p hc c", hc=16)
                                   [:, b4 * 4:(b4 + 1) * 4, ct * 128:(ct + 1) * 128])
                            src = pb[bank][:].rearrange("p (hc c) -> p hc c", hc=4)
                            P.emit("dve", lambda e, dst=dst, src=src, bank=bank: e.tensor_copy(
                                dst, src),
                                reads=[(f"pb{bank}",)], writes=[("gxf", ct)])
                gxfkeys = [("gxf", ct) for ct in range(CAP // 128)]
                for mc in range(8):
                    j = ex * 8 + mc
                    sl = j % 3
                    if routed:
                        bank = mc % 2
                        fns = [lambda e, kc=kc, bank=bank, sl=sl: e.matmul(
                            pb[bank][:, 0:CAP],
                            lhsT=ups[sl][:, kc * 128:(kc + 1) * 128],
                            rhs=gxf[:, kc * CAP:(kc + 1) * CAP],
                            start=(kc == 0), stop=(kc == 15)) for kc in range(16)]
                        P.emit_multi("pe", fns, reads=[("ups", sl)] + gxfkeys,
                                     writes=[(f"pb{bank}",)])
                    else:
                        bank0, bank1 = 2 * (mc % 2), 2 * (mc % 2) + 1
                        fns = []
                        for kc in range(16):
                            for half, bank in ((0, bank0), (1, bank1)):
                                fns.append(lambda e, kc=kc, half=half, bank=bank, sl=sl: e.matmul(
                                    pb[bank][:],
                                    lhsT=ups[sl][:, kc * 128:(kc + 1) * 128],
                                    rhs=xf2[:, kc * 1024 + half * 512: kc * 1024 + (half + 1) * 512],
                                    start=(kc == 0), stop=(kc == 15)))
                        P.emit_multi("pe", fns, reads=[("ups", sl)] + xf2keys,
                                     writes=[(f"pb{bank0}",), (f"pb{bank1}",)])
                    if j + 3 < len(upsched):
                        ne, nmc = upsched[j + 3]
                        up_dma(ne, nmc, (j + 3) % 3)
                    if mc == 2:
                        for dmc in range(4):
                            dn_dma(ex, dmc)
                    if mc == 5:
                        for dmc in range(4, 8):
                            dn_dma(ex, dmc)
                    if mc == 3 and ex + 1 < NEXP and ex + 1 >= NSH:
                        gate_fetch(ex + 1 - NSH)
                    if mc == 5 and NSH <= ex < NSH + NRT - 1:
                        gather_fetch(ex + 1 - NSH)
                    if routed:
                        bank = mc % 2
                        P.emit("act", lambda e, bank=bank: e.activation(
                            gsc[0][:, 0:CAP], pb[bank][:, 0:CAP], AF.Gelu),
                            reads=[(f"pb{bank}",)], writes=[("gsc", 0)])
                        P.emit("dve", lambda e, mc=mc, gsl=gsl: e.tensor_tensor(
                            rmid[:, mc * CAP:(mc + 1) * CAP], gsc[0][:, 0:CAP],
                            gb[gsl][:], ALU.mult),
                            reads=[("gsc", 0), ("gb", gsl)],
                            writes=[("rmid", mc)])
                    else:
                        for half, bank in ((0, bank0), (1, bank1)):
                            dst = (slice(None), slice(mc * 1024 + half * 512,
                                                      mc * 1024 + (half + 1) * 512))
                            P.emit("act", lambda e, bank=bank, dst=dst: e.activation(
                                mid[dst], pb[bank][:], AF.Gelu),
                                reads=[(f"pb{bank}",)], writes=[("mid", mc, half)])
                if DBG and ex == 0:
                    P.emit("sp", lambda e: e.dma_start(
                        out=ddbg_mid.ap(), in_=mid[:]),
                        reads=[("mid", mc, hf) for mc in range(8) for hf in range(2)],
                        kind="dma")
                if routed:
                    for ct in range(CAP // 128):
                        dbk = (4, 5, 6, 7) if ct % 2 == 0 else (0, 1, 2, 3)
                        fns = []
                        for mc in range(8):
                            for og in range(4):
                                fns.append(lambda e, ct=ct, mc=mc, og=og, b=dbk[og]: e.matmul(
                                    pb[b][:],
                                    lhsT=rmid[:, mc * CAP + ct * 128: mc * CAP + (ct + 1) * 128],
                                    rhs=dns[:, mc * D + og * 512: mc * D + (og + 1) * 512],
                                    start=(mc == 0), stop=(mc == 7)))
                        P.emit_multi("pe", fns,
                                     reads=[("rmid", mc) for mc in range(8)]
                                     + [("dns", mc) for mc in range(8)],
                                     writes=[(f"pb{b}",) for b in dbk])
                        for og in range(4):
                            bd = dbk[og]
                            if og % 2 == 0:
                                P.emit("act", lambda e, og=og, b=bd: e.activation(
                                    outg[:, og * 512:(og + 1) * 512], pb[b][:], AF.Copy),
                                    reads=[(f"pb{bd}",)], writes=[("outg", og)])
                            else:
                                P.emit("dve", lambda e, og=og, b=bd: e.tensor_copy(
                                    outg[:, og * 512:(og + 1) * 512], pb[b][:]),
                                    reads=[(f"pb{bd}",)], writes=[("outg", og)])
                        P.emit("sp", lambda e, r=r, ct=ct: e.dma_start(
                            out=doutd.ap()[r * CAP + ct * 128: r * CAP + (ct + 1) * 128, :],
                            in_=outg[:]),
                            reads=[("outg", og) for og in range(4)],
                            writes=[("outd", r, ct)], kind="dma")
                else:
                    for tt in range(8):
                        dbk = (4, 5, 6, 7) if tt % 2 == 0 else (0, 1, 2, 3)
                        fns = []
                        for mc in range(8):
                            for og in range(4):
                                fns.append(lambda e, tt=tt, mc=mc, og=og, b=dbk[og]: e.matmul(
                                    pb[b][:],
                                    lhsT=mid[:, mc * 1024 + tt * 128: mc * 1024 + (tt + 1) * 128],
                                    rhs=dns[:, mc * D + og * 512: mc * D + (og + 1) * 512],
                                    start=(mc == 0), stop=(mc == 7)))
                        P.emit_multi("pe", fns,
                                     reads=[("mid", mc, hf) for mc in range(8)
                                            for hf in range(2)]
                                     + [("dns", mc) for mc in range(8)],
                                     writes=[(f"pb{b}",) for b in dbk])
                        for og in range(4):
                            bd = dbk[og]
                            P.emit("dve", lambda e, tt=tt, og=og, b=bd: e.tensor_tensor(
                                x1[:, tt * D + og * 512: tt * D + (og + 1) * 512],
                                pb[b][:],
                                x1[:, tt * D + og * 512: tt * D + (og + 1) * 512],
                                ALU.add),
                                reads=[(f"pb{bd}",), ("x1", tt)],
                                writes=[("x1", tt)])

            outdkeys = [("outd", r, ct) for r in range(NRT) for ct in range(CAP // 128)]
            radd = outg
            outgkeys = [("outg", og) for og in range(4)]
            for i in range(8):
                for ki in range(2):
                    gi = g1i if ki == 0 else g2i
                    gkey = ("g1i" if ki == 0 else "g2i", i)
                    P.emit("pool", lambda e, i=i, gi=gi: e.indirect_dma_start(
                        out=radd[:], out_offset=None,
                        in_=doutd.ap(),
                        in_offset=IndirectOffsetOnAxis(gi[:, i:i + 1], 0)),
                        reads=[gkey, ("outd_trash",)] + outdkeys,
                        writes=[("radd",)] + outgkeys, kind="dma")
                    P.emit("dve", lambda e, i=i: e.tensor_tensor(
                        x1[:, i * D:(i + 1) * D], x1[:, i * D:(i + 1) * D],
                        radd[:], ALU.add),
                        reads=[("radd",), ("x1", i)], writes=[("x1", i)])

            for i in range(8):
                P.emit("sp", lambda e, i=i: e.dma_start(
                    out=dy.ap()[i * 128:(i + 1) * 128, :],
                    in_=x1[:, i * D:(i + 1) * D]),
                    reads=[("x1", i)], writes=[("y", i)], kind="dma")
            P.wait_only("sp", [(ln, v) for ln, v in P.dma.items()
                               if ln.startswith("dq_sp") and v > 0])
            P.flush(block)

    return nc


_NC_CACHE = {}


def _get_nc():
    if "nc" not in _NC_CACHE:
        _NC_CACHE["nc"] = build_nc()
    return _NC_CACHE["nc"]


def kernel(L0_attn_x, L0_attn_W_attn, L0_attn_W_proj, L0_attn_w_std,
           L0_attn_w_rec, L0_ffn_Wsh_up, L0_ffn_Wsh_down, L0_ffn_Wrt_up,
           L0_ffn_Wrt_down, L0_ffn_W_router):
    f32 = np.float32
    x2d = np.ascontiguousarray(np.asarray(L0_attn_x, f32).reshape(T, D))
    W_attn = np.asarray(L0_attn_W_attn, f32)
    W_proj = np.asarray(L0_attn_W_proj, f32)
    w_std = np.asarray(L0_attn_w_std, f32)
    w_rec = np.asarray(L0_attn_w_rec, f32)
    wshup = np.ascontiguousarray(np.asarray(L0_ffn_Wsh_up, f32))
    wshdn = np.ascontiguousarray(np.asarray(L0_ffn_Wsh_down, f32))
    wrtup = np.ascontiguousarray(np.asarray(L0_ffn_Wrt_up, f32))
    wrtdn = np.ascontiguousarray(np.asarray(L0_ffn_Wrt_down, f32))
    wrout = np.ascontiguousarray(np.asarray(L0_ffn_W_router, f32))

    ident_bf = np.eye(128, dtype=ml_dtypes.bfloat16)
    ident_f = np.eye(128, dtype=f32)
    ubar = np.zeros((128, 1), f32)
    ubar[DSTD:, 0] = 1.0
    sel = np.zeros((16, D), f32)
    for r in range(16):
        sel[r, r * 128:(r + 1) * 128] = 1.0
    tokio = (np.arange(8, dtype=np.int32)[None, :] * 128
             + np.arange(128, dtype=np.int32)[:, None]).astype(np.int32)
    lfill = np.full((NRT * CAP + 128, 1), TOWN, np.int32)
    gzfill = np.zeros((NRT * CAP + 128, 1), f32)
    erow = (np.arange(16, dtype=f32) * CAP - 1.0).reshape(16, 1)
    masks = np.zeros((4, 128, 512), f32)
    kk = np.arange(128)[:, None]
    qq = np.arange(512)[None, :]
    for j in range(4):
        masks[j] = np.where(j * 128 + kk <= qq, 0.0, NEG).astype(f32)

    q_std0, k_std0, v0, q_low0, k_low0 = 0, 1920, 3840, 5888, 6016

    def qcols(h):
        return (list(range(q_std0 + h * DSTD, q_std0 + (h + 1) * DSTD))
                + list(range(q_low0 + h * RLOW, q_low0 + (h + 1) * RLOW)))

    def kcols(h):
        return (list(range(k_std0 + h * DSTD, k_std0 + (h + 1) * DSTD))
                + list(range(k_low0 + h * RLOW, k_low0 + (h + 1) * RLOW)))

    def vcols(h):
        return list(range(v0 + h * HD, v0 + (h + 1) * HD))

    in_maps = []
    for c in range(NCORE):
        h0, h1 = 2 * c, 2 * c + 1
        order = (qcols(h0) + kcols(h0) + qcols(h1) + kcols(h1)
                 + vcols(h0) + vcols(h1))
        assert len(order) == 768
        wqkv_c = np.ascontiguousarray(W_attn[order, :].T)
        cdims = list(range(h0 * HD, (h0 + 1) * HD)) + list(range(h1 * HD, (h1 + 1) * HD))
        wproj_c = np.ascontiguousarray(W_proj.T[cdims, :])
        in_maps.append({
            "x": x2d,
            "x_own": np.ascontiguousarray(np.concatenate(
                [x2d[512 * q0 + 64 * nq * c: 512 * q0 + 64 * nq * (c + 1)]
                 for q0, nq in zip(CHUNK_Q0, CHUNK_QT)], axis=0)),
            "wqkv": wqkv_c,
            "wproj": wproj_c,
            "wstd": np.ascontiguousarray(w_std[[h0, h1]].reshape(1, 2)),
            "wrec": np.ascontiguousarray(w_rec[[h0, h1]].reshape(1, 2)),
            "wshup": wshup, "wshdn": wshdn,
            "wrtup": wrtup, "wrtdn": wrtdn, "wrout": wrout,
            "ident_bf": ident_bf, "ident_f": ident_f,
            "ubar": ubar, "sel": sel, "masks": masks,
            "tokio": tokio, "lfill": lfill, "gzfill": gzfill, "erow": erow,
        })

    nc = _get_nc()
    res = run_bass_kernel_spmd(nc, in_maps, core_ids=list(range(NCORE)))
    out = np.empty((T, D), np.float32)
    for c in range(NCORE):
        yc = np.asarray(res.results[c]["y"])
        for q0, nq in zip(CHUNK_Q0, CHUNK_QT):
            out[512 * q0 + 64 * nq * c: 512 * q0 + 64 * nq * (c + 1)] =                 yc[64 * q0: 64 * (q0 + nq)]
    return out.reshape(1, T, D)


if __name__ == "__main__":
    nc = build_nc()
    n = sum(1 for _ in nc.all_instructions())
    print("built, instructions:", n)
